# version 03
# baseline (speedup 1.0000x reference)
import math
from contextlib import ExitStack

import numpy as np

import concourse.bass as bass
import concourse.mybir as mybir
from concourse.bass_utils import run_bass_kernel_spmd

F32 = mybir.dt.float32
BF16 = mybir.dt.bfloat16
AF = mybir.ActivationFunctionType
ALU = mybir.AluOpType
AX = mybir.AxisListType

B, T, D = 2, 4096, 2048
H, DK = 16, 128
EXP, IFF = 16, 1408
NE = 18
NCORE = 8
TPC = 1024
NT = B * T
SCALE = 1.0 / math.sqrt(DK)
EPS = 1e-5
NEGINF = -1.0e30
NDC = D // 128
NIC = IFF // 128

ENGS = ("sync", "gpsimd", "scalar", "vector", "tensor")


class Res:

    def __init__(self, prog):
        self.p = prog
        self.w = {}
        self.r = {}

    def read(self, eng):
        for s, v in self.w.items():
            self.p.wait(eng, s, v)

    def write(self, eng):
        for s, v in self.w.items():
            self.p.wait(eng, s, v)
        for s, v in self.r.items():
            self.p.wait(eng, s, v)

    def setw(self, sem, val):
        if val is not None:
            self.w[sem] = max(self.w.get(sem, 0), val)

    def setr(self, sem, val):
        if val is not None:
            self.r[sem] = max(self.r.get(sem, 0), val)


class Prog:
    def __init__(self, nc, sems):
        self.nc = nc
        self.sems = sems
        self.cnt = {k: 0 for k in sems}
        self.waited = {e: {} for e in ENGS}
        self.fns = {e: [] for e in ENGS}

    def wait(self, eng, sem, val, force=False):
        if val is None or val <= 0:
            return
        if not force and self.waited[eng].get(sem, 0) >= val:
            return
        self.waited[eng][sem] = max(self.waited[eng].get(sem, 0), val)
        h = self.sems[sem]
        self.fns[eng].append(lambda e, h=h, v=val: e.wait_ge(h, v))

    def op(self, eng, make, sem=None, amt=1):
        if sem is None:
            self.fns[eng].append(make)
            return None
        self.cnt[sem] += amt
        tgt = self.cnt[sem]
        h = self.sems[sem]
        self.fns[eng].append(lambda e, make=make, h=h, a=amt: make(e).then_inc(h, a))
        return tgt

    def stream(self, name):
        if name not in self.sems:
            self.sems[name] = self.nc.alloc_semaphore(name=name)
            self.cnt[name] = 0
        return name

    def dma(self, eng, stream, out, in_, reads=(), writes=()):
        sem = self.stream(stream)
        for r in reads:
            r.read(eng)
        for w in writes:
            w.write(eng)
        v = self.op(eng, lambda e, o=out, i=in_: e.dma_start(out=o, in_=i), sem, 16)
        for r in reads:
            r.setr(sem, v)
        for w in writes:
            w.setw(sem, v)
        return sem, v

    def join_all(self):
        snap = dict(self.cnt)
        for e in ENGS:
            for s, v in snap.items():
                self.wait(e, s, v)


def build_nc():
    nc = bass.Bass("TRN2", target_bir_lowering=False)
    st = ExitStack()

    def din(name, shape):
        return nc.dram_tensor(name, list(shape), F32, kind="ExternalInput").ap()

    xfm = din("xfm", (D, NT))
    x_own = din("x_own", (TPC, D))
    qkvw = din("qkvw", (D, 768))
    wproj = din("wproj", (256, D))
    gatew = din("gatew", (D, EXP))
    w13 = din("w13", (NE, D, NIC, 256))
    w2 = din("w2", (NE, IFF, D))
    iden_in = din("iden", (128, 128))
    masks_in = din("masks", (128, 4, 512))

    out = nc.dram_tensor("out", [TPC, D], F32, kind="ExternalOutput").ap()

    q_scr = nc.dram_tensor("q_scr", [2, 2, 2, 128, T], BF16).ap()
    k_scr = nc.dram_tensor("k_scr", [2, 2, 2, 128, T], BF16).ap()
    v_scr = nc.dram_tensor("v_scr", [2, 2, 2, T, 128], BF16).ap()
    rs_in = nc.dram_tensor("rs_in", [NT, D], F32).ap()
    rs_out = nc.dram_tensor("rs_out", [TPC, D], F32).ap()
    w13b0 = nc.dram_tensor("w13b0", [NE // 2, 2, D, NIC, 256], BF16).ap()
    w13b1 = nc.dram_tensor("w13b1", [NE - NE // 2, 2, D, NIC, 256], BF16).ap()

    def w13b_at(e_i):
        return (w13b0, e_i) if e_i < NE // 2 else (w13b1, e_i - NE // 2)
    w2b = nc.dram_tensor("w2b", [NE, 2, IFF, D], BF16).ap()

    sems = {}
    for name in ("pe", "act", "dve", "cc"):
        sems[name] = st.enter_context(nc.semaphore(name))
    p = Prog(nc, sems)

    iden = st.enter_context(nc.sbuf_tensor("iden_sb", [128, 128], F32))
    masks = st.enter_context(nc.sbuf_tensor("masks_sb", [128, 4, 512], F32))
    small = st.enter_context(nc.sbuf_tensor("small", [128, 32], F32))
    gsc = st.enter_context(nc.sbuf_tensor("gsc", [128, 80], F32))
    combine = st.enter_context(nc.sbuf_tensor("combine", [128, 4, EXP], F32))

    pb = [st.enter_context(nc.psum_tensor(f"pb{i}", [128, 512], F32)) for i in range(8)]
    pbr = [Res(p) for _ in range(8)]

    def ser(eng, make, sem):
        p.wait(eng, "dve", p.cnt["dve"])
        p.wait(eng, "act", p.cnt["act"])
        return p.op(eng, make, sem=sem)

    def split(src, hi, lo, src_res, dst_res, hi_eng="scalar"):
        src_res.read(hi_eng)
        dst_res.write(hi_eng)
        if hi_eng == "scalar":
            av = p.op("scalar", lambda e, o=hi, i=src:
                      e.activation(out=o, in_=i, func=AF.Copy), sem="act")
            hsem = "act"
        else:
            av = p.op(hi_eng, lambda e, o=hi, i=src: e.tensor_copy(o, i), sem="dve")
            hsem = "dve"
        src_res.setr(hsem, av)
        dst_res.setw(hsem, av)
        src_res.read("vector")
        dst_res.write("vector")
        p.wait("vector", hsem, av)
        lv = p.op("vector", lambda e, o=lo, a=src, b=hi:
                  e.tensor_tensor(o, a, b, op=ALU.subtract), sem="dve")
        src_res.setr("dve", lv)
        dst_res.setw("dve", lv)
        return dst_res

    def mm3(bank_ap, bank_res, chunks, reads):
        bank_res.write("tensor")
        for r in reads:
            r.read("tensor")
        n = len(chunks)
        for i, (lh, ll, rh, rl) in enumerate(chunks):
            for j, (a, bb) in enumerate(((lh, rh), (lh, rl), (ll, rh))):
                first = (i == 0 and j == 0)
                last = (i == n - 1 and j == 2)
                p.op("tensor",
                     lambda e, o=bank_ap, l=a, r=bb, st_=first, sp=last:
                         e.matmul(o, l, r, start=st_, stop=sp),
                     sem="pe" if last else None)
        v = p.cnt["pe"]
        bank_res.setw("pe", v)
        for r in reads:
            r.setr("pe", v)
        return v

    r_iden, r_masks = Res(p), Res(p)
    r_eps = Res(p)
    ev0 = p.op("vector", lambda e: e.memset(small[:, 17:18], EPS), sem="dve")
    r_eps.setw("dve", ev0)
    p.dma("sync", "c_iden", iden[:, :], iden_in[:, :], writes=[r_iden])
    p.dma("sync", "c_masks", masks[:, :, :], masks_in[:, :, :], writes=[r_masks])

    r_q = [[Res(p) for _ in range(2)] for _ in range(2)]
    r_k = [[Res(p) for _ in range(2)] for _ in range(2)]
    r_v = [[Res(p) for _ in range(2)] for _ in range(2)]
    r_rsin = Res(p)
    r_rsout = Res(p)
    r_w13b = [Res(p) for _ in range(NE)]
    r_w2b = [Res(p) for _ in range(NE)]

    ph1 = ExitStack()
    wqs = ph1.enter_context(nc.sbuf_tensor("wqs", [128, 768], F32))
    wq_h = ph1.enter_context(nc.sbuf_tensor("wq_h", [128, NDC, 768], BF16))
    wq_l = ph1.enter_context(nc.sbuf_tensor("wq_l", [128, NDC, 768], BF16))
    xts = ph1.enter_context(nc.sbuf_tensor("xts", [128, NDC, 256], F32))
    xt_h = ph1.enter_context(nc.sbuf_tensor("xt_h", [128, 2, NDC, 512], BF16))
    xt_l = ph1.enter_context(nc.sbuf_tensor("xt_l", [128, 2, NDC, 512], BF16))
    qk_tmp = ph1.enter_context(nc.sbuf_tensor("qk_tmp", [128, 2, 4, 512], BF16))
    v_tmp = ph1.enter_context(nc.sbuf_tensor("v_tmp", [128, 2, 2, 4, 256], BF16))

    r_wqs, r_wq = Res(p), Res(p)
    r_xts = Res(p)
    r_xt = [Res(p), Res(p)]
    r_qk = [Res(p) for _ in range(4)]
    r_vt = [Res(p), Res(p)]

    for dc in range(NDC):
        p.dma("sync", "wqs", wqs[:, :], qkvw[dc * 128:(dc + 1) * 128, :],
              reads=[r_wqs], writes=[r_wqs])
        split(wqs[:, :], wq_h[:, dc, :], wq_l[:, dc, :], r_wqs, r_wq)

    NSL = NT // 512
    for sl in range(NSL):
        buf = sl % 2
        for hseg in range(2):
            colg = sl * 512 + hseg * 256
            p.dma("sync", "xts", xts[:, :, :],
                  xfm[:, colg:colg + 256].rearrange("(c p) n -> p c n", p=128),
                  reads=[r_xts], writes=[r_xts])
            split(xts[:, :, :], xt_h[:, buf, :, hseg * 256:hseg * 256 + 256],
                  xt_l[:, buf, :, hseg * 256:hseg * 256 + 256], r_xts, r_xt[buf])
        s, col = sl // 8, (sl % 8) * 512
        for unit in range(4):
            hh = unit % 2
            wcol = (unit // 2) * 256 + hh * 128
            bank = unit
            chunks = [(wq_h[:, dc, wcol:wcol + 128], wq_l[:, dc, wcol:wcol + 128],
                       xt_h[:, buf, dc, :], xt_l[:, buf, dc, :]) for dc in range(NDC)]
            mm3(pb[bank][:, :], pbr[bank], chunks, [r_wq, r_xt[buf]])
            split(pb[bank][:, :], qk_tmp[:, 0, unit, :], qk_tmp[:, 1, unit, :],
                  pbr[bank], r_qk[unit])
            scr = (q_scr if unit < 2 else k_scr)[s, hh, :, :, col:col + 512]
            rres = (r_q if unit < 2 else r_k)[s][hh]
            r_qk[unit].read("gpsimd")
            gv = p.dma("gpsimd", f"qk{unit}", scr.rearrange("h p t -> p h t"),
                       qk_tmp[:, :, unit, :], writes=[rres])
            r_qk[unit].setr(*gv)
        vbuf = sl % 2
        for tt in range(4):
            bank = 4 + tt % 2
            chunks = [(xt_h[:, buf, dc, tt * 128:(tt + 1) * 128],
                       xt_l[:, buf, dc, tt * 128:(tt + 1) * 128],
                       wq_h[:, dc, 512:768], wq_l[:, dc, 512:768]) for dc in range(NDC)]
            mm3(pb[bank][:, 0:256], pbr[bank], chunks, [r_wq, r_xt[buf]])
            split(pb[bank][:, 0:256], v_tmp[:, vbuf, 0, tt, :], v_tmp[:, vbuf, 1, tt, :],
                  pbr[bank], r_vt[vbuf])
        for hh in range(2):
            for hl in range(2):
                r_vt[vbuf].read("gpsimd")
                gv = p.dma("gpsimd", f"vt{vbuf}{hh}{hl}",
                           v_scr[s, hh, hl, col:col + 512, :].rearrange("(t p) n -> p t n", p=128),
                           v_tmp[:, vbuf, hl, :, hh * 128:hh * 128 + 128],
                           writes=[r_v[s][hh]])
                r_vt[vbuf].setr(*gv)

    p.join_all()
    ph1.close()

    ph2 = ExitStack()
    K_sb = ph2.enter_context(nc.sbuf_tensor("K_sb", [128, 2, T], BF16))
    V_sb = ph2.enter_context(nc.sbuf_tensor("V_sb", [128, 2, 32, 128], BF16))
    Q_sb = ph2.enter_context(nc.sbuf_tensor("Q_sb", [128, 2, T], BF16))
    S_sb = ph2.enter_context(nc.sbuf_tensor("S_sb", [128, 2, T], F32))
    pt_sb = ph2.enter_context(nc.sbuf_tensor("pt_sb", [128, 2, 2, 256], BF16))
    A_sb = ph2.enter_context(nc.sbuf_tensor("A_sb", [128, 2, 2, T], BF16))
    wps = ph2.enter_context(nc.sbuf_tensor("wps", [128, 2, D], F32))
    wp_h = ph2.enter_context(nc.sbuf_tensor("wp_h", [128, 2, D], BF16))
    wp_l = ph2.enter_context(nc.sbuf_tensor("wp_l", [128, 2, D], BF16))
    ptmp = ph2.enter_context(nc.sbuf_tensor("ptmp", [128, 2, D], F32))
    pps = ph2.enter_context(nc.sbuf_tensor("pps", [128, 2816], F32))
    ppo = ph2.enter_context(nc.sbuf_tensor("ppo", [128, 2, 2, 2816], BF16))

    rK, rV, rQ = Res(p), Res(p), Res(p)
    rS = [Res(p), Res(p)]
    rPT = [Res(p), Res(p)]
    rA = [Res(p), Res(p)]
    rWPs, rWP = Res(p), Res(p)
    rPtmp = [Res(p), Res(p)]
    rNM = [Res(p), Res(p)]
    rRD = [Res(p), Res(p)]
    rPPS = Res(p)
    rPPO = [Res(p), Res(p)]

    p.dma("sync", "wps", wps[:, :, :], wproj.rearrange("(h p) n -> p h n", p=128),
          writes=[rWPs])
    split(wps[:, :, :], wp_h[:, :, :], wp_l[:, :, :], rWPs, rWP)

    def prepass_pieces():
        for e_i in range(NE):
            for ic in range(NIC):
                for wc in range(2):
                    yield ("w13", e_i, ic, wc)
            for q4 in range(8):
                yield ("w2", e_i, q4)

    _pp = prepass_pieces()
    _pp_done = [False]
    _pp_idx = [0]

    def drain_prepass(n):
        for _ in range(n):
            piece = next(_pp, None)
            if piece is None:
                _pp_done[0] = True
                return
            dbuf = _pp_idx[0] % 2
            _pp_idx[0] += 1
            if piece[0] == "w13":
                _, e_i, ic, wc = piece
                nch, nn = NDC, 128
                src = w13[e_i, :, ic, wc * 128:(wc + 1) * 128]
                _t, _e = w13b_at(e_i)
                dst = _t[_e, :, :, ic, wc * 128:(wc + 1) * 128]
                res = r_w13b[e_i]
            else:
                _, e_i, q4 = piece
                nch, nn = NIC, 256
                src = w2[e_i, :, q4 * 256:(q4 + 1) * 256]
                dst = w2b[e_i, :, :, q4 * 256:(q4 + 1) * 256]
                res = r_w2b[e_i]
            nel = nch * nn
            p.dma("sync", "pps", pps[:, 0:nel].rearrange("p (c n) -> p c n", c=nch),
                  src.rearrange("(c p) n -> p c n", p=128),
                  reads=[rPPS], writes=[rPPS])
            split(pps[:, 0:nel], ppo[:, dbuf, 0, 0:nel], ppo[:, dbuf, 1, 0:nel],
                  rPPS, rPPO[dbuf])
            for hl in range(2):
                rPPO[dbuf].read("gpsimd")
                gv = p.dma("gpsimd", f"ppw{dbuf}{hl}",
                           dst[hl].rearrange("(c p) n -> p c n", p=128),
                           ppo[:, dbuf, hl, 0:nel].rearrange("p (c n) -> p c n", c=nch),
                           writes=[res])
                rPPO[dbuf].setr(*gv)

    NQT = T // 128

    for s in range(2):
        for hh in range(2):
            p.dma("sync", "sK", K_sb[:, :, :],
                  k_scr[s, hh, :, :, :].rearrange("h p t -> p h t"),
                  reads=[r_k[s][hh]], writes=[rK])
            for hl in range(2):
                p.dma("sync", f"sV{hl}", V_sb[:, hl, :, :],
                      v_scr[s, hh, hl, :, :].rearrange("(c p) n -> p c n", p=128),
                      reads=[r_v[s][hh]], writes=[rV])
            p.dma("sync", "sQ", Q_sb[:, :, :],
                  q_scr[s, hh, :, :, :].rearrange("h p t -> p h t"),
                  reads=[r_q[s][hh]], writes=[rQ])

            for j in range(NQT):
                nk = (j + 4) // 4
                W = nk * 512
                slot = j % 2
                for kc in range(nk):
                    bank = (j * 8 + kc) % 2
                    chunks = [(Q_sb[:, 0, j * 128:(j + 1) * 128],
                               Q_sb[:, 1, j * 128:(j + 1) * 128],
                               K_sb[:, 0, kc * 512:(kc + 1) * 512],
                               K_sb[:, 1, kc * 512:(kc + 1) * 512])]
                    mm3(pb[bank][:, :], pbr[bank], chunks, [rQ, rK])
                    pbr[bank].read("scalar")
                    rS[slot].write("scalar")
                    av = p.op("scalar", lambda e, o=S_sb[:, slot, kc * 512:(kc + 1) * 512],
                              i=pb[bank][:, :]: e.activation(out=o, in_=i, func=AF.Copy,
                                                             scale=SCALE), sem="act")
                    pbr[bank].setr("act", av)
                    rS[slot].setw("act", av)
                rS[slot].read("vector")
                r_masks.read("vector")
                dv = p.op("vector", lambda e, o=S_sb[:, slot, (j // 4) * 512:(j // 4) * 512 + 512],
                          m=masks[:, j % 4, :]: e.tensor_tensor(o, o, m, op=ALU.add), sem="dve")
                rS[slot].setw("dve", dv)
                rNM[slot].write("vector")
                rS[slot].read("vector")
                p.wait("vector", "dve", p.cnt["dve"])
                dv = p.op("vector", lambda e, o=small[:, slot:slot + 1], i=S_sb[:, slot, 0:W]:
                          e.reduce_max(o, i, axis=AX.X, negate=True), sem="dve")
                rS[slot].setr("dve", dv)
                rNM[slot].setw("dve", dv)
                rS[slot].write("scalar")
                rNM[slot].read("scalar")
                av = p.op("scalar", lambda e, o=S_sb[:, slot, 0:W], b=small[:, slot:slot + 1],
                          a=small[:, 4 + slot:5 + slot]:
                          e.activation(out=o, in_=o, func=AF.Exp, bias=b, scale=1.0,
                                       accum_out=a), sem="act")
                rS[slot].setw("act", av)
                rNM[slot].setr("act", av)
                rRD[slot].write("vector")
                dv = ser("vector", lambda e, o=small[:, 2 + slot:3 + slot],
                         i=small[:, 4 + slot:5 + slot]: e.reciprocal(o, i), sem="dve")
                rRD[slot].setw("dve", dv)
                rS[slot].write("vector")
                dv = ser("vector", lambda e, o=S_sb[:, slot, 0:W], r=small[:, 2 + slot:3 + slot]:
                         e.tensor_scalar_mul(o, o, r), sem="dve")
                rS[slot].setw("dve", dv)

                if j % 2 == 1:
                    g = j // 2
                    ng = 2 * (g + 1)
                    pvbank = 4
                    pbr[pvbank].write("tensor")
                    rV.read("tensor")
                    for tkc in range(ng):
                        tb = 2 + tkc % 2
                        pbr[tb].write("tensor")
                        rS[0].read("tensor")
                        rS[1].read("tensor")
                        r_iden.read("tensor")
                        p.op("tensor", lambda e, o=pb[tb][:, 0:128],
                             i=S_sb[:, 0, tkc * 128:(tkc + 1) * 128], d=iden[:, :]:
                             e.transpose(o, i, d))
                        v = p.op("tensor", lambda e, o=pb[tb][:, 128:256],
                                 i=S_sb[:, 1, tkc * 128:(tkc + 1) * 128], d=iden[:, :]:
                                 e.transpose(o, i, d), sem="pe")
                        pbr[tb].setw("pe", v)
                        rS[0].setr("pe", v)
                        rS[1].setr("pe", v)
                        ptb = tkc % 2
                        split(pb[tb][:, 0:256], pt_sb[:, ptb, 0, :], pt_sb[:, ptb, 1, :],
                              pbr[tb], rPT[ptb])
                        rPT[ptb].read("tensor")
                        for j2, (a, bb) in enumerate((
                                (V_sb[:, 0, tkc, :], pt_sb[:, ptb, 0, :]),
                                (V_sb[:, 0, tkc, :], pt_sb[:, ptb, 1, :]),
                                (V_sb[:, 1, tkc, :], pt_sb[:, ptb, 0, :]))):
                            first = (tkc == 0 and j2 == 0)
                            last = (tkc == ng - 1 and j2 == 2)
                            p.op("tensor",
                                 lambda e, o=pb[pvbank][:, 0:256], l=a, r=bb, st_=first,
                                 sp=last: e.matmul(o, l, r, start=st_, stop=sp),
                                 sem="pe" if last else None)
                        vv = p.cnt["pe"]
                        rPT[ptb].setr("pe", vv)
                        rV.setr("pe", vv)
                    pbr[pvbank].setw("pe", p.cnt["pe"])
                    split(pb[pvbank][:, 0:256], A_sb[:, hh, 0, g * 256:(g + 1) * 256],
                          A_sb[:, hh, 1, g * 256:(g + 1) * 256], pbr[pvbank], rA[hh])
                    drain_prepass(3)

        for tt in range(NQT):
            pbuf = tt % 2
            rPtmp[pbuf].write("scalar")
            for dsl in range(4):
                bank = 5 + (tt * 4 + dsl) % 2
                chunks = [(A_sb[:, hh2, 0, tt * 128:(tt + 1) * 128],
                           A_sb[:, hh2, 1, tt * 128:(tt + 1) * 128],
                           wp_h[:, hh2, dsl * 512:(dsl + 1) * 512],
                           wp_l[:, hh2, dsl * 512:(dsl + 1) * 512]) for hh2 in range(2)]
                mm3(pb[bank][:, :], pbr[bank], chunks, [rA[0], rA[1], rWP])
                pbr[bank].read("scalar")
                av = p.op("scalar", lambda e, o=ptmp[:, pbuf, dsl * 512:(dsl + 1) * 512],
                          i=pb[bank][:, :]: e.activation(out=o, in_=i, func=AF.Copy), sem="act")
                pbr[bank].setr("act", av)
            rPtmp[pbuf].setw("act", p.cnt["act"])
            rPtmp[pbuf].read("gpsimd")
            gv = p.dma("gpsimd", f"pt{pbuf}", rs_in[s * T + tt * 128: s * T + (tt + 1) * 128, :],
                       ptmp[:, pbuf, :], writes=[r_rsin])
            rPtmp[pbuf].setr(*gv)
            drain_prepass(2)

    while not _pp_done[0]:
        drain_prepass(8)

    p.join_all()
    ph2.close()

    r_rsin.read("gpsimd")
    r_rsout.write("gpsimd")
    cv = p.op("gpsimd", lambda e: e.collective_compute(
        "ReduceScatter", ALU.add, replica_groups=[list(range(NCORE))],
        ins=[rs_in[:, :]], outs=[rs_out[:, :]]), sem="cc")
    r_rsin.setr("cc", cv)
    r_rsout.setw("cc", cv)

    ph4 = ExitStack()
    x1_h = ph4.enter_context(nc.sbuf_tensor("x1_h", [128, NDC, 512], BF16))
    x1_l = ph4.enter_context(nc.sbuf_tensor("x1_l", [128, NDC, 512], BF16))
    y_sb = ph4.enter_context(nc.sbuf_tensor("y_sb", [128, 4, D], F32))
    g_h = ph4.enter_context(nc.sbuf_tensor("g_h", [128, NIC, 512], BF16))
    g_l = ph4.enter_context(nc.sbuf_tensor("g_l", [128, NIC, 512], BF16))
    g32 = ph4.enter_context(nc.sbuf_tensor("g32", [128, 512], F32))
    w13_sb = ph4.enter_context(nc.sbuf_tensor("w13_sb", [128, 2, 2, NDC, 256], BF16))
    w2_sb = ph4.enter_context(nc.sbuf_tensor("w2_sb", [128, 2, 2, NIC, 512], BF16))
    xo_sb = ph4.enter_context(nc.sbuf_tensor("xo_sb", [128, D], F32))
    ro_sb = ph4.enter_context(nc.sbuf_tensor("ro_sb", [128, D], F32))
    gws = ph4.enter_context(nc.sbuf_tensor("gws", [128, NDC, EXP], F32))
    gw_h = ph4.enter_context(nc.sbuf_tensor("gw_h", [128, NDC, EXP], BF16))
    gw_l = ph4.enter_context(nc.sbuf_tensor("gw_l", [128, NDC, EXP], BF16))
    s1_sb = ph4.enter_context(nc.sbuf_tensor("s1_sb", [128, 512], F32))

    rX1, rY, rG, rG32 = Res(p), Res(p), Res(p), Res(p)
    rW13 = [Res(p), Res(p)]
    rW2 = [Res(p), Res(p)]
    rXO, rRO = Res(p), Res(p)
    rGWs, rGW = Res(p), Res(p)
    rS1 = Res(p)
    rSml = Res(p)
    rComb = Res(p)

    p.dma("sync", "c_gw", gws[:, :, :], gatew.rearrange("(c p) n -> p c n", p=128),
          writes=[rGWs])
    split(gws[:, :, :], gw_h[:, :, :], gw_l[:, :, :], rGWs, rGW)

    for pa in range(2):
        tok0 = pa * 512
        for tt in range(4):
            row = tok0 + tt * 128
            p.dma("sync", "sxo", xo_sb[:, :], x_own[row:row + 128, :],
                  reads=[rXO], writes=[rXO])
            p.dma("sync", "sro", ro_sb[:, :], rs_out[row:row + 128, :],
                  reads=[r_rsout, rRO], writes=[rRO])
            rXO.read("vector")
            rRO.read("vector")
            rY.write("vector")
            dv = p.op("vector", lambda e, o=y_sb[:, tt, :], a=xo_sb[:, :], b=ro_sb[:, :]:
                      e.tensor_add(o, a, b), sem="dve")
            rXO.setr("dve", dv)
            rRO.setr("dve", dv)
            rY.setw("dve", dv)
            rY.read("vector")
            rSml.write("vector")
            ser("vector", lambda e, o=small[:, 8:9], i=y_sb[:, tt, :]:
                e.reduce_sum(o, i, axis=AX.X), sem="dve")
            dv = ser("vector", lambda e, o=small[:, 8:9]:
                     e.tensor_scalar_mul(o, o, 1.0 / D), sem="dve")
            rSml.setw("dve", dv)
            rY.read("scalar")
            rSml.read("scalar")
            rRO.write("scalar")
            av = ser("scalar", lambda e, o=ro_sb[:, :], i=y_sb[:, tt, :], a=small[:, 9:10]:
                     e.activation(out=o, in_=i, func=AF.Square, accum_out=a), sem="act")
            rRO.setw("act", av)
            rSml.setw("act", av)
            rY.setr("act", av)
            rSml.write("vector")
            ser("vector", lambda e, o=small[:, 10:11], i=small[:, 9:10]:
                e.tensor_scalar_mul(o, i, 1.0 / D), sem="dve")
            ser("vector", lambda e, o=small[:, 11:12], a=small[:, 8:9], b=small[:, 8:9]:
                e.tensor_mul(o, a, b), sem="dve")
            dv = ser("vector", lambda e, o=small[:, 10:11], a=small[:, 10:11],
                     b=small[:, 11:12]: e.tensor_sub(o, a, b), sem="dve")
            rSml.setw("dve", dv)
            rSml.read("scalar")
            r_eps.read("scalar")
            av = ser("scalar", lambda e, o=small[:, 12:13], i=small[:, 10:11],
                     b=small[:, 17:18]: e.activation(out=o, in_=i, func=AF.Sqrt, bias=b,
                                                     scale=1.0), sem="act")
            rSml.setw("act", av)
            ser("vector", lambda e, o=small[:, 13:14], i=small[:, 12:13]:
                e.reciprocal(o, i), sem="dve")
            ser("vector", lambda e, o=small[:, 14:15], a=small[:, 13:14], b=small[:, 13:14]:
                e.tensor_mul(o, a, b), sem="dve")
            ser("vector", lambda e, o=small[:, 15:16], a=small[:, 10:11]:
                e.tensor_scalar(o, a, EPS, None, ALU.add), sem="dve")
            ser("vector", lambda e, o=small[:, 14:15], a=small[:, 14:15], b=small[:, 15:16]:
                e.tensor_mul(o, a, b), sem="dve")
            ser("vector", lambda e, o=small[:, 14:15], a=small[:, 14:15]:
                e.tensor_scalar(o, a, -0.5, 1.5, ALU.mult, ALU.add), sem="dve")
            ser("vector", lambda e, o=small[:, 13:14], a=small[:, 13:14], b=small[:, 14:15]:
                e.tensor_mul(o, a, b), sem="dve")
            dv = ser("vector", lambda e, o=small[:, 16:17], a=small[:, 8:9], b=small[:, 13:14]:
                     e.scalar_tensor_tensor(o, a, -1.0, b, ALU.mult, ALU.mult), sem="dve")
            rSml.setw("dve", dv)
            rY.write("vector")
            rSml.read("vector")
            av = ser("vector", lambda e, o=y_sb[:, tt, :], b=small[:, 16:17],
                     s_=small[:, 13:14]: e.tensor_scalar(o, o, s_, b, ALU.mult, ALU.add),
                     sem="dve")
            rY.setw("dve", av)
            rSml.setr("dve", av)
            for dc in range(NDC):
                bank = 7 if dc % 2 == 0 else 3
                pbr[bank].write("tensor")
                rY.read("tensor")
                r_iden.read("tensor")
                v = p.op("tensor", lambda e, o=pb[bank][:, 0:128],
                         i=y_sb[:, tt, dc * 128:(dc + 1) * 128], d=iden[:, :]:
                         e.transpose(o, i, d), sem="pe")
                pbr[bank].setw("pe", v)
                rY.setr("pe", v)
                split(pb[bank][:, 0:128], x1_h[:, dc, tt * 128:(tt + 1) * 128],
                      x1_l[:, dc, tt * 128:(tt + 1) * 128], pbr[bank], rX1)

        for tt in range(4):
            bank = 7
            chunks = [(x1_h[:, dc, tt * 128:(tt + 1) * 128],
                       x1_l[:, dc, tt * 128:(tt + 1) * 128],
                       gw_h[:, dc, :], gw_l[:, dc, :]) for dc in range(NDC)]
            mm3(pb[bank][:, 0:EXP], pbr[bank], chunks, [rX1, rGW])
            pbr[bank].read("scalar")
            rComb.write("scalar")
            av = p.op("scalar", lambda e, o=gsc[:, 0:16], i=pb[bank][:, 0:EXP]:
                      e.activation(out=o, in_=i, func=AF.Sigmoid), sem="act")
            pbr[bank].setr("act", av)
            rComb.setw("act", av)

            def gop(make):
                rComb.write("vector")
                dv = ser("vector", make, sem="dve")
                rComb.setw("dve", dv)
                return dv
            gop(lambda e: e.reduce_max(gsc[:, 16:20],
                gsc[:, 0:16].rearrange("p (a b) -> p a b", a=4), axis=AX.X))
            gop(lambda e: e.reduce_max(gsc[:, 24:25], gsc[:, 16:20], axis=AX.X))
            gop(lambda e: e.tensor_scalar(gsc[:, 20:24], gsc[:, 16:20], gsc[:, 24:25], -2.0,
                                          ALU.is_ge, ALU.mult))
            gop(lambda e: e.tensor_add(gsc[:, 20:24], gsc[:, 20:24], gsc[:, 16:20]))
            gop(lambda e: e.reduce_max(gsc[:, 24:25], gsc[:, 20:24], axis=AX.X))
            gop(lambda e: e.tensor_scalar(gsc[:, 16:20], gsc[:, 16:20], gsc[:, 24:25], None,
                                          ALU.is_ge))
            gop(lambda e: e.tensor_tensor(gsc[:, 25:41].rearrange("p (a b) -> p a b", a=4),
                                          gsc[:, 0:16].rearrange("p (a b) -> p a b", a=4),
                                          gsc[:, 16:20].unsqueeze(2).broadcast_to([128, 4, 4]),
                                          op=ALU.mult))
            gop(lambda e: e.tensor_copy(gsc[:, 41:57], gsc[:, 25:41]))
            for _ in range(3):
                gop(lambda e: e.reduce_max(gsc[:, 24:25], gsc[:, 41:57], axis=AX.X))
                gop(lambda e: e.tensor_scalar(gsc[:, 57:73], gsc[:, 41:57], gsc[:, 24:25],
                                              -2.0, ALU.is_ge, ALU.mult))
                gop(lambda e: e.tensor_add(gsc[:, 41:57], gsc[:, 41:57], gsc[:, 57:73]))
            gop(lambda e: e.reduce_max(gsc[:, 24:25], gsc[:, 41:57], axis=AX.X))
            gop(lambda e: e.scalar_tensor_tensor(gsc[:, 41:57], gsc[:, 25:41], gsc[:, 24:25],
                                                 gsc[:, 25:41], ALU.is_ge, ALU.mult))
            gop(lambda e: e.reduce_sum(gsc[:, 73:74], gsc[:, 41:57], axis=AX.X))
            gop(lambda e: e.reciprocal(gsc[:, 74:75], gsc[:, 73:74]))
            rComb.write("vector")
            dv = ser("vector", lambda e, o=combine[:, tt, :], a=gsc[:, 41:57],
                     r=gsc[:, 74:75]: e.tensor_scalar_mul(o, a, r), sem="dve")
            rComb.setw("dve", dv)

        for e_i in range(NE):
            for ic in range(NIC):
                wbuf = ic % 2
                _t, _e = w13b_at(e_i)
                for hl in range(2):
                    p.dma("sync", f"w13{wbuf}{hl}", w13_sb[:, wbuf, hl, :, :],
                          _t[_e, hl, :, ic, :].rearrange("(c p) n -> p c n", p=128),
                          reads=[r_w13b[e_i]], writes=[rW13[wbuf]])
                b1, b3 = (0, 1) if ic % 2 == 0 else (2, 3)
                for bank, wc in ((b1, 0), (b3, 128)):
                    chunks = [(w13_sb[:, wbuf, 0, dc, wc:wc + 128],
                               w13_sb[:, wbuf, 1, dc, wc:wc + 128],
                               x1_h[:, dc, :], x1_l[:, dc, :]) for dc in range(NDC)]
                    mm3(pb[bank][:, :], pbr[bank], chunks, [rW13[wbuf], rX1])
                pbr[b1].read("scalar")
                rS1.write("scalar")
                av = p.op("scalar", lambda e, o=s1_sb[:, :], i=pb[b1][:, :]:
                          e.activation(out=o, in_=i, func=AF.Silu), sem="act")
                pbr[b1].setr("act", av)
                rS1.setw("act", av)
                rS1.read("vector")
                pbr[b3].read("vector")
                rG32.write("vector")
                p.wait("vector", "act", av)
                dv = p.op("vector", lambda e, o=g32[:, :], a=s1_sb[:, :], i=pb[b3][:, :]:
                          e.tensor_mul(o, a, i), sem="dve")
                rS1.setr("dve", dv)
                pbr[b3].setr("dve", dv)
                rG32.setw("dve", dv)
                split(g32[:, :], g_h[:, ic, :], g_l[:, ic, :], rG32, rG)
            for dsl in range(4):
                wbuf = dsl % 2
                for hl in range(2):
                    p.dma("sync", f"w2{wbuf}{hl}", w2_sb[:, wbuf, hl, :, :],
                          w2b[e_i, hl, :, dsl * 512:(dsl + 1) * 512].rearrange(
                              "(c p) n -> p c n", p=128),
                          reads=[r_w2b[e_i]], writes=[rW2[wbuf]])
                for tt in range(4):
                    bank = 4 + (dsl * 4 + tt) % 2
                    chunks = [(g_h[:, ic, tt * 128:(tt + 1) * 128],
                               g_l[:, ic, tt * 128:(tt + 1) * 128],
                               w2_sb[:, wbuf, 0, ic, :],
                               w2_sb[:, wbuf, 1, ic, :]) for ic in range(NIC)]
                    mm3(pb[bank][:, :], pbr[bank], chunks, [rW2[wbuf], rG])
                    pbr[bank].read("vector")
                    rY.write("vector")
                    if e_i < EXP:
                        rComb.read("vector")
                        dv = p.op("vector", lambda e, o=y_sb[:, tt, dsl * 512:(dsl + 1) * 512],
                                  i=pb[bank][:, :], c=combine[:, tt, e_i:e_i + 1]:
                                  e.scalar_tensor_tensor(o, i, c, o, ALU.mult, ALU.add),
                                  sem="dve")
                        rComb.setr("dve", dv)
                    else:
                        dv = p.op("vector", lambda e, o=y_sb[:, tt, dsl * 512:(dsl + 1) * 512],
                                  i=pb[bank][:, :]: e.tensor_add(o, o, i), sem="dve")
                    pbr[bank].setr("dve", dv)
                    rY.setw("dve", dv)

        for tt in range(4):
            row = tok0 + tt * 128
            rY.read("vector")
            rSml.write("vector")
            ser("vector", lambda e, o=small[:, 8:9], i=y_sb[:, tt, :]:
                e.reduce_sum(o, i, axis=AX.X), sem="dve")
            dv = ser("vector", lambda e, o=small[:, 8:9]:
                     e.tensor_scalar_mul(o, o, 1.0 / D), sem="dve")
            rSml.setw("dve", dv)
            rY.setr("dve", dv)
            rY.read("scalar")
            rSml.read("scalar")
            buf_r = rXO if tt % 2 == 0 else rRO
            buf_t = xo_sb if tt % 2 == 0 else ro_sb
            buf_r.write("scalar")
            av = ser("scalar", lambda e, o=buf_t[:, :], i=y_sb[:, tt, :], a=small[:, 9:10]:
                     e.activation(out=o, in_=i, func=AF.Square, accum_out=a), sem="act")
            buf_r.setw("act", av)
            rSml.setw("act", av)
            rY.setr("act", av)
            rSml.write("vector")
            ser("vector", lambda e, o=small[:, 10:11], i=small[:, 9:10]:
                e.tensor_scalar_mul(o, i, 1.0 / D), sem="dve")
            ser("vector", lambda e, o=small[:, 11:12], a=small[:, 8:9], b=small[:, 8:9]:
                e.tensor_mul(o, a, b), sem="dve")
            dv = ser("vector", lambda e, o=small[:, 10:11], a=small[:, 10:11],
                     b=small[:, 11:12]: e.tensor_sub(o, a, b), sem="dve")
            rSml.setw("dve", dv)
            rSml.read("scalar")
            r_eps.read("scalar")
            av = ser("scalar", lambda e, o=small[:, 12:13], i=small[:, 10:11],
                     b=small[:, 17:18]: e.activation(out=o, in_=i, func=AF.Sqrt, bias=b,
                                                     scale=1.0), sem="act")
            rSml.setw("act", av)
            ser("vector", lambda e, o=small[:, 13:14], i=small[:, 12:13]:
                e.reciprocal(o, i), sem="dve")
            ser("vector", lambda e, o=small[:, 14:15], a=small[:, 13:14], b=small[:, 13:14]:
                e.tensor_mul(o, a, b), sem="dve")
            ser("vector", lambda e, o=small[:, 15:16], a=small[:, 10:11]:
                e.tensor_scalar(o, a, EPS, None, ALU.add), sem="dve")
            ser("vector", lambda e, o=small[:, 14:15], a=small[:, 14:15], b=small[:, 15:16]:
                e.tensor_mul(o, a, b), sem="dve")
            ser("vector", lambda e, o=small[:, 14:15], a=small[:, 14:15]:
                e.tensor_scalar(o, a, -0.5, 1.5, ALU.mult, ALU.add), sem="dve")
            ser("vector", lambda e, o=small[:, 13:14], a=small[:, 13:14], b=small[:, 14:15]:
                e.tensor_mul(o, a, b), sem="dve")
            dv = ser("vector", lambda e, o=small[:, 16:17], a=small[:, 8:9], b=small[:, 13:14]:
                     e.scalar_tensor_tensor(o, a, -1.0, b, ALU.mult, ALU.mult), sem="dve")
            rSml.setw("dve", dv)
            rY.read("vector")
            buf_r.write("vector")
            rSml.read("vector")
            av = ser("vector", lambda e, o=buf_t[:, :], i=y_sb[:, tt, :], b=small[:, 16:17],
                     s_=small[:, 13:14]: e.tensor_scalar(o, i, s_, b, ALU.mult, ALU.add),
                     sem="dve")
            buf_r.setw("dve", av)
            rSml.setr("dve", av)
            rY.setr("dve", av)
            buf_r.read("gpsimd")
            gv = p.dma("gpsimd", f"sout{tt % 2}", out[row:row + 128, :], buf_t[:, :])
            buf_r.setr(*gv)

    for s_ in list(p.cnt):
        if s_.startswith(("sout", "qk", "vt", "pt", "ppw")):
            p.wait("gpsimd", s_, p.cnt[s_], force=True)

    with nc.Block() as block:
        for eng in ENGS:
            getattr(block, eng)(lambda e, fns=p.fns[eng]: [f(e) for f in fns])

    ph4.close()
    st.close()
    return nc


_NC_CACHE = None


def _get_nc():
    global _NC_CACHE
    if _NC_CACHE is None:
        _NC_CACHE = build_nc()
    return _NC_CACHE


def make_in_maps(L0_attn_x, L0_attn_W_qkv, L0_attn_W_proj, L0_ffn_gate_w,
                 L0_ffn_W1, L0_ffn_W2, L0_ffn_W3, L0_ffn_Ws1, L0_ffn_Ws2, L0_ffn_Ws3):
    x = np.ascontiguousarray(np.asarray(L0_attn_x, dtype=np.float32)).reshape(NT, D)
    Wqkv = np.asarray(L0_attn_W_qkv, dtype=np.float32)
    Wproj = np.asarray(L0_attn_W_proj, dtype=np.float32)
    gate_w = np.asarray(L0_ffn_gate_w, dtype=np.float32)
    W1 = np.asarray(L0_ffn_W1, dtype=np.float32)
    W2 = np.asarray(L0_ffn_W2, dtype=np.float32)
    W3 = np.asarray(L0_ffn_W3, dtype=np.float32)
    Ws1 = np.asarray(L0_ffn_Ws1, dtype=np.float32)
    Ws2 = np.asarray(L0_ffn_Ws2, dtype=np.float32)
    Ws3 = np.asarray(L0_ffn_Ws3, dtype=np.float32)

    xfm = np.ascontiguousarray(x.T)
    WqkvT = np.ascontiguousarray(Wqkv.T)
    WprojT = np.ascontiguousarray(Wproj.T)
    gatewT = np.ascontiguousarray(gate_w.T)

    w13 = np.empty((NE, D, NIC, 256), dtype=np.float32)
    W1T = np.ascontiguousarray(W1.transpose(0, 2, 1))
    W3T = np.ascontiguousarray(W3.transpose(0, 2, 1))
    w13[:EXP, :, :, 0:128] = W1T.reshape(EXP, D, NIC, 128)
    w13[:EXP, :, :, 128:256] = W3T.reshape(EXP, D, NIC, 128)
    Ws1T = np.ascontiguousarray(Ws1.T).reshape(D, 2, NIC, 128)
    Ws3T = np.ascontiguousarray(Ws3.T).reshape(D, 2, NIC, 128)
    for half in range(2):
        w13[EXP + half, :, :, 0:128] = Ws1T[:, half]
        w13[EXP + half, :, :, 128:256] = Ws3T[:, half]

    w2all = np.empty((NE, IFF, D), dtype=np.float32)
    w2all[:EXP] = W2.transpose(0, 2, 1)
    Ws2T = np.ascontiguousarray(Ws2.T)
    w2all[EXP] = Ws2T[:IFF]
    w2all[EXP + 1] = Ws2T[IFF:]
    w2all = np.ascontiguousarray(w2all)

    iden = np.eye(128, dtype=np.float32)
    masks = np.zeros((128, 4, 512), dtype=np.float32)
    pidx = np.arange(128)[:, None]
    kidx = np.arange(512)[None, :]
    for j in range(4):
        masks[:, j, :] = np.where(kidx <= 128 * j + pidx, 0.0, NEGINF)

    in_maps = []
    for c in range(NCORE):
        h0 = 2 * c
        qkvw = np.ascontiguousarray(np.concatenate(
            [WqkvT[:, h0 * 128:(h0 + 2) * 128],
             WqkvT[:, D + h0 * 128:D + (h0 + 2) * 128],
             WqkvT[:, 2 * D + h0 * 128:2 * D + (h0 + 2) * 128]], axis=1))
        in_maps.append({
            "xfm": xfm,
            "x_own": np.ascontiguousarray(x[c * TPC:(c + 1) * TPC]),
            "qkvw": qkvw,
            "wproj": np.ascontiguousarray(WprojT[h0 * 128:(h0 + 2) * 128, :]),
            "gatew": gatewT,
            "w13": w13,
            "w2": w2all,
            "iden": iden,
            "masks": masks,
        })
    return in_maps


def kernel(**inputs):
    in_maps = make_in_maps(**inputs)
    nc = _get_nc()
    res = run_bass_kernel_spmd(nc, in_maps, core_ids=list(range(NCORE)))
    full = np.concatenate([res.results[c]["out"] for c in range(NCORE)], axis=0)
    return full.reshape(B, T, D).astype(np.float32, copy=False)
